# version 01 (final)
# baseline (speedup 1.0000x reference)
import numpy as np
import ml_dtypes
import concourse.bass as bass
import concourse.mybir as mybir
from concourse.bass_utils import run_bass_kernel_spmd

F32 = mybir.dt.float32
BF16 = mybir.dt.bfloat16
I32 = mybir.dt.int32
U8 = mybir.dt.uint8
ALU = mybir.AluOpType
AF = mybir.ActivationFunctionType
AX = mybir.AxisListType

NCORE = 8
T = 2048
D = 1024
NT = 16
DI = 4096
H = 16
HD = 64
CD = 3 * D
POS = T // NCORE
VELT_C = float(2 ** 22 + 1)
LN448 = float(np.log(np.float64(448.0)))
CS_K2 = float(np.float32(0.02) / np.float32(32.0))

BUILD_UPTO = "all"
DEBUG_TAPS = []
NT_LIM = NT
TRUNC = None
SKIP_WQUANT = False


class SB:

    def __init__(self, nc):
        self.nc = nc
        self.sems = {}
        self.cnt = {}
        self.unit = {}
        self.engine_of = {}
        self.queues = {"tensor": [], "scalar": [], "vector": [], "gpsimd": [], "sync": []}
        self.last_w = {}
        self.last_r = {}
        self.watermark = {}
        self._ctxs = []

    def add_stream(self, name, engine, unit):
        ctx = self.nc.semaphore("sem_" + name)
        self.sems[name] = ctx.__enter__()
        self._ctxs.append(ctx)
        self.cnt[name] = 0
        self.unit[name] = unit
        self.engine_of[name] = engine

    def _deps(self, stream, reads, writes):
        need = {}

        def add(ps, v, kind):
            if ps == stream:
                if stream == "dve":
                    pass
                elif self.unit[stream] == 16 and kind != "waw":
                    pass
                else:
                    return
            if v > self.watermark.get((stream, ps), 0) and v > need.get(ps, 0):
                need[ps] = v

        for k in reads:
            w = self.last_w.get(k)
            if w:
                add(*w, "raw")
        for k in writes:
            w = self.last_w.get(k)
            if w:
                add(*w, "waw")
            for ps, v in self.last_r.get(k, {}).items():
                add(ps, v, "war")
        for ps, v in need.items():
            self.watermark[(stream, ps)] = v
        return list(need.items())

    def op(self, stream, fn, reads=(), writes=()):
        waits = self._deps(stream, reads, writes)
        self.cnt[stream] += self.unit[stream]
        myval = self.cnt[stream]
        sem = self.sems[stream]
        u = self.unit[stream]
        sems = self.sems

        def closure(eng, waits=waits, fn=fn):
            for ps, v in waits:
                eng.wait_ge(sems[ps], v)
            fn(eng).then_inc(sem, u)

        self.gidx = getattr(self, "gidx", 0) + 1
        self.queues[self.engine_of[stream]].append((self.gidx, closure))
        for k in reads:
            self.last_r.setdefault(k, {})[stream] = myval
        for k in writes:
            self.last_w[k] = (stream, myval)
            self.last_r[k] = {}
        return myval

    def global_barrier(self):
        sems = self.sems
        snap = {s: self.cnt[s] for s in self.cnt}
        for s in self.cnt:
            vals = [(ps, v) for ps, v in snap.items()
                    if ps != s and v > 0 and v > self.watermark.get((s, ps), 0)]
            for ps, v in vals:
                self.watermark[(s, ps)] = v
            if vals:
                def closure(eng, vals=vals):
                    for ps, v in vals:
                        eng.wait_ge(sems[ps], v)
                self.gidx = getattr(self, "gidx", 0) + 1
                self.queues[self.engine_of[s]].append((self.gidx, closure))

    def final_wait(self, host_engine, on_stream):
        v = self.cnt[on_stream]
        sems = self.sems
        self.gidx = getattr(self, "gidx", 0) + 1
        self.queues[host_engine].append(
            (self.gidx, lambda eng: eng.wait_ge(sems[on_stream], v)))

    def replay(self, block, trunc=None):
        for engname, q in self.queues.items():
            if not q:
                continue

            def runner(eng, q=q):
                for idx, c in q:
                    if trunc is not None and idx > trunc:
                        continue
                    c(eng)

            getattr(block, engname)(runner)


class Arena:

    def __init__(self, ap, nelem):
        self.ap = ap
        self.nelem = nelem
        self.ptr = 0
        self.marks = {}

    def carve(self, name, n):
        assert self.ptr + n <= self.nelem, \
            f"arena overflow: {name} needs {n}, have {self.nelem - self.ptr}"
        sl = self.ap[:, self.ptr:self.ptr + n]
        self.marks[name] = (self.ptr, n)
        self.ptr += n
        return sl

    def free_to(self, name):
        self.ptr = self.marks[name][0]
        for k in [k for k, (o, _) in self.marks.items() if o >= self.ptr]:
            del self.marks[k]


class K:
    def __init__(self):
        self.nc = bass.Bass("TRN2", target_bir_lowering=False, detect_race_conditions=False)
        self.sb = SB(self.nc)
        for name, eng, unit in (("pe", "tensor", 1), ("act", "scalar", 1),
                                ("dve", "vector", 1), ("pool", "gpsimd", 1),
                                ("cc", "gpsimd", 1), ("qs", "sync", 16),
                                ("qa", "scalar", 16), ("qp", "gpsimd", 16)):
            self.sb.add_stream(name, eng, unit)
        self.taps = {}

    def din(self, name, shape, dtype=F32):
        return self.nc.dram_tensor(name, list(shape), dtype, kind="ExternalInput").ap()

    def dout(self, name, shape, dtype=F32):
        return self.nc.dram_tensor(name, list(shape), dtype, kind="ExternalOutput").ap()

    def dram(self, name, shape, dtype=F32, shared=False):
        return self.nc.dram_tensor(name, list(shape), dtype, kind="Internal",
                                   addr_space="Shared" if shared else "Local").ap()

    def sbuf(self, name, shape, dtype=F32):
        return self.nc.alloc_sbuf_tensor(name, list(shape), dtype).ap()

    def psum(self, name, shape, dtype=F32):
        return self.nc.alloc_psum_tensor(name, list(shape), dtype).ap()

    def dma(self, q, out, in_, reads=(), writes=()):
        self.sb.op(q, lambda e: e.dma_start(out=out, in_=in_), reads=reads, writes=writes)

    def cp(self, st, out, in_, reads=(), writes=()):
        if st == "act":
            self.sb.op(st, lambda e: e.copy(out, in_), reads=reads, writes=writes)
        else:
            self.sb.op(st, lambda e: e.tensor_copy(out, in_), reads=reads, writes=writes)

    def tt(self, st, out, a, b, op, reads=(), writes=()):
        self.sb.op(st, lambda e: e.tensor_tensor(out, a, b, op), reads=reads, writes=writes)

    def ts(self, st, out, a, s1, s2, op0, op1=ALU.bypass, reads=(), writes=()):
        if s2 is None:
            self.sb.op(st, lambda e: e.tensor_single_scalar(out, a, s1, op0),
                       reads=reads, writes=writes)
        else:
            self.sb.op(st, lambda e: e.tensor_scalar(out, a, s1, s2, op0, op1),
                       reads=reads, writes=writes)

    def stt(self, st, out, a, s, b, op0, op1, reads=(), writes=()):
        self.sb.op(st, lambda e: e.scalar_tensor_tensor(out, a, s, b, op0, op1),
                   reads=reads, writes=writes)

    def sel(self, out, mask, tv, fv, reads=(), writes=()):
        self.sb.op("dve", lambda e: e.select(out, mask, tv, fv), reads=reads, writes=writes)

    def act(self, out, in_, func, bias=0.0, scale=1.0, accum_out=None, reads=(), writes=()):
        reads = list(reads)
        if isinstance(bias, float) and bias != 0.0:
            reads.append("cbias")
        self.sb.op("act", lambda e: e.activation(out, in_, func, bias=bias, scale=scale,
                                                 accum_out=accum_out),
                   reads=reads, writes=writes)

    def red(self, st, out, in_, op, absval=False, reads=(), writes=()):
        self.sb.op(st, lambda e: e.tensor_reduce(out, in_, AX.X, op,
                                                 apply_absolute_value=absval),
                   reads=reads, writes=writes)

    def memset(self, st, ap, v, writes=()):
        self.sb.op(st, lambda e: e.memset(ap, v), writes=writes)

    def mmg(self, mms, reads=(), writes=()):
        def fn(eng):
            ins = None
            for (o, l, r, st_, sp) in mms:
                ins = self.nc.tensor.matmul(o, l, r, start=st_, stop=sp,
                                            skip_group_check=True)
            return ins

        return self.sb.op("pe", fn, reads=reads, writes=writes)

    def tr(self, out_psum, in_sbuf, bf, reads=(), writes=()):
        ident = self.c_identB if bf else self.c_identF
        self.sb.op("pe", lambda e: self.nc.tensor.transpose(out_psum, in_sbuf, ident),
                   reads=list(reads) + ["consts"], writes=writes)

    def hilo(self, st, src, hi, lo, rk, wk):
        self.cp(st, hi, src, reads=[rk], writes=[wk])
        self.tt(st, lo, src, hi, ALU.subtract, reads=[rk, wk], writes=[wk])

    def sigmoid(self, out, outk, x, xk, t0, t0k, t1, t1k, st="dve", scale=None):
        if scale is None:
            self.ts(st, t0, x, 0.0, None, ALU.min, reads=[xk], writes=[t0k])
            self.stt(st, t1, t0, 2.0, x, ALU.mult, ALU.subtract,
                     reads=[xk, t0k], writes=[t1k])
        else:
            self.ts(st, t0, x, scale, 0.0, ALU.mult, ALU.min, reads=[xk],
                    writes=[t0k])
            self.ts(st, t1, x, scale, 0.0, ALU.mult, ALU.max, reads=[xk],
                    writes=[t1k])
            self.tt(st, t1, t0, t1, ALU.subtract, reads=[t0k, t1k],
                    writes=[t1k])
        self.act(t1, t1, AF.Exp, reads=[t1k], writes=[t1k])
        self.act(t1, t1, AF.Ln, bias=1.0, reads=[t1k], writes=[t1k])
        self.tt(st, t1, t0, t1, ALU.subtract, reads=[t0k, t1k], writes=[t1k])
        self.act(out, t1, AF.Exp, reads=[t1k], writes=[outk])

    def phase(self, name):
        self.sb.global_barrier()

    def _start_flush(self):
        nc = self.nc
        ksr = nc._kernel_sem_range
        lo = ksr.start + 2
        if nc._bir_kernel_barrier_sem is not None:
            lo += 1
        lo += len(nc._monotonic_sems)
        rng = range(lo, ksr.stop)
        nc.all_engine_barrier()
        for _ in range(2):
            nc.gpsimd.dma_reset(rng)
            nc.gpsimd.sem_clear(rng)
            nc.all_engine_barrier()

    def build(self):
        self._start_flush()
        self.declare_io()
        self.alloc()
        self.prologue()
        if not SKIP_WQUANT:
            self.wquant_phase()
        stage = {"l0attn": 0, "l0ffn": 1, "l1norm": 2, "l1attn": 3,
                 "l1ffn": 4, "all": 5}[BUILD_UPTO]
        self.l0_attn_phase()
        if stage >= 1:
            self.l0_ffn_phase()
            self.l0_combine_phase()
        if stage >= 2:
            self.l1_norm_phase()
        if stage >= 3:
            self.l1_attn_phase()
        if stage >= 4:
            self.l1_ffn_phase()
        if stage >= 5:
            self.l1_final_phase()
        self.epilogue()
        with self.nc.Block() as block:
            self.sb.replay(block, trunc=TRUNC)
        self.nc.reset()
        return self.nc

    def declare_io(self):
        d = self.din
        self.x_in = d("x", [T, D])
        self.Wq1T = d("Wq1T", [D, D]); self.bq1 = d("bq1", [1, D])
        self.Wk1T = d("Wk1T", [D, D]); self.bk1 = d("bk1", [1, D])
        self.Wk2T = d("Wk2T", [D, D]); self.bk2 = d("bk2", [1, D])
        self.WprojT = d("WprojT", [D, D]); self.bproj = d("bproj", [1, D])
        self.gate_sh = d("gate_sh", [DI // 8, D])
        self.up_sh = d("up_sh", [DI // 8, D])
        self.down_sh = d("down_sh", [D // 8, DI])
        self.s_in = d("s_in", [1, 1]); self.s_in_down = d("s_in_down", [1, 1])
        self.WqkvT = d("WqkvT", [D, CD])
        self.convw = d("convw", [CD, 4])
        self.WzT = d("WzT", [D, D])
        self.WabT = d("WabT", [D, 2 * H])
        self.dt_bias = d("dt_bias", [1, H]); self.A_log = d("A_log", [1, H])
        self.WoutT = d("WoutT", [D, D])
        for n in ("wf", "wi", "wc", "wo", "uf", "ui", "uc", "uo"):
            setattr(self, n + "T", d(n + "T", [D, D]))
        for n in ("bf", "bi", "bc", "bo", "in_c"):
            setattr(self, n + "_in", d("i_" + n, [1, D]))
        self.identF = d("identF", [128, 128])
        self.shdB = d("shdB", [128, 128], BF16)
        self.pickB = d("pickB", [128, 4], BF16)
        self.identB = d("identB", [128, 128], BF16)
        self.triuB = d("triuB", [128, 128], BF16)
        self.causalF = d("causalF", [128, 128])
        self.out = self.dout("out", [T, D])
        for tp in DEBUG_TAPS:
            shp = {"s0": [T, D], "f0": [T, D], "x1": [T, D], "xn1T": [D, T],
                   "s1": [T, D], "g1": [T, D], "K": [T, D], "S": [T, D],
                   "Q": [T, D], "E": [T, D], "S2": [T, D], "xqT": [D, T],
                   "mixed": [CD, T], "beta": [T, H], "gdec": [T, H],
                   "o_raw": [T, D], "cfin": [POS, D], "xn0": [T, D],
                   "lnm": [128, NT], "lnr": [128, NT], "lncol": [128, 8],
                   "xtile": [T, D],
                   "wqg": [D, DI], "wqu": [D, DI], "wqd": [DI, D]}[tp]
            dt_ = BF16 if tp in ("xn1T", "xqT", "wqg", "wqu", "wqd", "mixed") else F32
            self.taps[tp] = self.dout("tap_" + tp, shp, dt_)
        self.s0_dram = self.dram("d_s0", [T, D])
        self.g0_dram = self.dram("d_g0", [T, D])
        self.s1_dram = self.dram("d_s1", [T, D])
        self.x1_dram = self.dram("d_x1", [T, D])
        self.ag_xn_in = self.dram("ag_xn_in", [D, T], BF16)
        self.ag_xn_out = self.dram("ag_xn_out", [NCORE, D, T], BF16, shared=True)
        self.rs_in = self.dram("rs_in", [NCORE, NCORE, D, POS], BF16)
        self.rs_out = self.dram("rs_out", [NCORE, D, POS], BF16)
        self.ag_cuo_in = self.dram("ag_cuo_in", [POS, D])
        self.ag_cuo_out = self.dram("ag_cuo_out", [NCORE, POS, D], shared=True)
        self.ar_in = self.dram("ar_in", [4, 1])
        self.ar_out = self.dram("ar_out", [4, 1], shared=True)
        self.agw_in = {"gate": self.dram("agwi_g", [D, DI // 8], BF16),
                       "up": self.dram("agwi_u", [D, DI // 8], BF16),
                       "down": self.dram("agwi_d", [DI, D // 8], BF16)}
        self.agw_out = {"gate": self.dram("agwo_g", [NCORE, D, DI // 8], BF16, shared=True),
                        "up": self.dram("agwo_u", [NCORE, D, DI // 8], BF16, shared=True),
                        "down": self.dram("agwo_d", [NCORE, DI, D // 8], BF16, shared=True)}

    def alloc(self):
        s, p = self.sbuf, self.psum
        self.c_identF = s("c_identF", [128, 128])
        self.c_identB = s("c_identB", [128, 128], BF16)
        self.c_shdB = s("c_shdB", [128, 128], BF16)
        self.c_pick = s("c_pick", [128, 4], BF16)
        self.e0row = s("e0row", [1, 128], BF16)
        self.c_triuB = s("c_triuB", [128, 128], BF16)
        self.ones_row = s("ones_row", [2, 128], BF16)
        self.ln_m = s("ln_m", [128, NT])
        self.ln_r = s("ln_r", [128, NT])
        self.col1 = s("col1", [128, 8])
        bb = s("biasall", [2, 4 * D], BF16)
        bnames = ("bq1", "bk1", "bk2", "bproj", "bf", "bi", "bc", "bo")
        self.bias_rows = {n: bb[0:2, (i % 4) * D:(i % 4 + 1) * D]
                          for i, n in enumerate(bnames)}
        self.dt_rows = s("br_dt", [2, H], BF16)
        cb = s("carryall", [1, 10 * D], BF16)
        rows = [cb[0:1, c * D:(c + 1) * D] for c in range(10)]
        (self.carK_hi, self.carK_lo, self.carKV_hi, self.carKV_lo,
         self.carS2_hi, self.carS2_lo, self.prow_oar_hi, self.prow_oar_lo,
         self.prow_Q_hi, self.prow_Q_lo) = rows
        self.scr = [s(f"scr{i}", [128, D]) for i in range(8)]
        self.scrb = [s(f"scrb{i}", [128, D], BF16) for i in range(4)]
        self.xtile = s("xtile", [128, D])
        self.qsc = [s(f"qsc{i}", [128, 64]) for i in range(7)]
        self.qmask = s("qmask", [128, D], U8)
        self.gs6col = s("gs6col", [128, 2])
        self.lnscol = s("lnscol", [128, 2])
        self.gw6col = s("gw6col", [128, 3])
        self.lnwcol = s("lnwcol", [128, 3])
        self.invcol = s("invcol", [128, 3])
        self.c16 = [s(f"c16_{i}", [128, H]) for i in range(2)]
        self.l1c = s("l1c", [128, 8 * H])
        self.Ltok = s("Ltok", [128, 1024], BF16)
        self.Rtok = s("Rtok", [128, 1024], BF16)
        self.Lpk = s("Lpk", [128, 1024], BF16)
        self.Rpk = s("Rpk", [128, 1024], BF16)
        self.c_ones128 = s("c_ones128", [128, 128], BF16)
        self.c_maskB = s("c_maskB", [128, 128], BF16)
        self.S_f = s("S_f", [128, 512])
        self.S_bf = s("S_bf", [128, 512], BF16)
        self.convw_sb = s("convw_sb", [128, 24 * 4])
        self.cbias = s("cbias", [128, 4])
        for i, v in enumerate((1e-05, 1e-06, 1.0)):
            self.nc.const_aps.register(F32, v, self.cbias[:, i:i + 1])
        ABF_N = 51200
        AF_N = 3072
        self.abf = Arena(s("arena_bf", [128, ABF_N], BF16), ABF_N)
        self.af = Arena(s("arena_f32", [128, AF_N]), AF_N)
        self.wstage = [self.af.carve(f"wstage{i}", 1024) for i in range(2)]
        self.srow = self.af.carve("srow", 1024)[0:1, :]
        self.pb = [p(f"pb{i}", [128, 512]) for i in range(6)]
        self.pbt = p("pbt", [128, 1024], BF16)
        self.pbc = p("pbc", [128, 512])

    def prologue(self):
        self.dma("qs", self.c_identF, self.identF, writes=["consts"])
        self.dma("qs", self.c_identB, self.identB, writes=["consts"])
        self.dma("qs", self.c_triuB, self.triuB, writes=["consts"])
        self.dma("qs", self.c_shdB, self.shdB, writes=["consts"])
        self.dma("qs", self.c_pick, self.pickB, writes=["consts"])
        self.memset("dve", self.e0row, 0.0, writes=["consts"])
        self.memset("dve", self.e0row[0:1, 0:1], 1.0, writes=["consts"])
        self.memset("dve", self.ones_row, 1.0, writes=["ones"])
        self.memset("dve", self.c_ones128, 1.0, writes=["consts"])
        self.ts("dve", self.c_maskB, self.c_triuB, 30000.0, -30000.0,
                ALU.mult, ALU.add, reads=["consts"], writes=["consts"])
        for i, v in enumerate((1e-05, 1e-06, 1.0)):
            self.memset("dve", self.cbias[:, i:i + 1], v, writes=["cbias"])
        for n, srcd in (("bq1", self.bq1), ("bk1", self.bk1), ("bk2", self.bk2),
                        ("bproj", self.bproj)):
            self.dma("qs", self.srow, srcd, reads=["srow"], writes=["srow"])
            self.cp("dve", self.scrb[0][0:1, :], self.srow, reads=["srow"],
                    writes=["b0"])
            self.tt("dve", self.scrb[1][0:1, :], self.srow, self.scrb[0][0:1, :],
                    ALU.subtract, reads=["srow", "b0"], writes=["b1"])
            slot = "bsl" + str(("bq1", "bk1", "bk2", "bproj").index(n))
            self.dma("qs", self.bias_rows[n][0:1, :], self.scrb[0][0:1, :],
                     reads=["b0"], writes=[slot])
            self.dma("qs", self.bias_rows[n][1:2, :], self.scrb[1][0:1, :],
                     reads=["b1"], writes=[slot])
        self.dma("qs", self.srow[0:1, 0:H], self.dt_bias, reads=["srow"],
                 writes=["srow"])
        self.cp("dve", self.scrb[0][0:1, 0:H], self.srow[0:1, 0:H], reads=["srow"],
                writes=["b0"])
        self.tt("dve", self.scrb[1][0:1, 0:H], self.srow[0:1, 0:H],
                self.scrb[0][0:1, 0:H], ALU.subtract, reads=["srow", "b0"],
                writes=["b1"])
        self.dma("qs", self.dt_rows[0:1, :], self.scrb[0][0:1, 0:H], reads=["b0"],
                 writes=["br_dt"])
        self.dma("qs", self.dt_rows[1:2, :], self.scrb[1][0:1, 0:H], reads=["b1"],
                 writes=["br_dt"])
        self.dma("qs", self.gs6col[:, 0:1], self.s_in.broadcast_to((128, 1)),
                 writes=["gs6"])
        self.dma("qs", self.gs6col[:, 1:2], self.s_in_down.broadcast_to((128, 1)),
                 writes=["gs6"])
        self.act(self.lnscol, self.gs6col, AF.Ln, reads=["gs6"], writes=["lns"])
        self.ts("dve", self.gs6col, self.gs6col, float(1.0 / 6.0), None, ALU.mult,
                reads=["gs6"], writes=["gs6"])

    def load_w(self, dram_ap, sbuf_bf, key, rows, cols, q="qa", st="act"):
        for c in range(rows // 128):
            stg = self.wstage[c % 2]
            sk = f"wstage{c % 2}"
            self.dma(q, stg[:, 0:cols], dram_ap[c * 128:(c + 1) * 128, :], writes=[sk])
            self.cp(st, sbuf_bf[:, c * cols:(c + 1) * cols], stg[:, 0:cols],
                    reads=[sk], writes=[key])

    def quant_tile(self, src, srck, out_bf, outk, W, gs6_col, lngs_col, sc_keys,
                   t0, t0k, t1, t1k, t2, t2k):
        NB = W // 16
        q0, q1, q2, q3 = (self.qsc[i][:, 0:NB] for i in range(4))
        src3 = src.rearrange("p (b j) -> p b j", j=16)
        self.red("dve", q0, src3, ALU.max, absval=True, reads=[srck], writes=["q0"])
        self.ts("dve", q1, q0, gs6_col, 448.0, ALU.mult, ALU.min,
                reads=["q0"] + list(sc_keys), writes=["q1"])
        self.ts("dve", self.qsc[4][:, 0:NB].bitcast(I32), q1.bitcast(I32),
                0x7F800000, None, ALU.bitwise_and, reads=["q1"], writes=["q4"])
        self.ts("dve", q2, self.qsc[4][:, 0:NB], 0.125, float(2.0 ** -9),
                ALU.mult, ALU.max, reads=["q4"], writes=["q2"])
        self.ts("dve", self.qsc[5][:, 0:NB].bitcast(I32), q2.bitcast(I32),
                -1, None, ALU.bitwise_xor, reads=["q2"], writes=["q5"])
        self.ts("dve", self.qsc[5][:, 0:NB].bitcast(I32),
                self.qsc[5][:, 0:NB].bitcast(I32), 0x7F000001, None, ALU.add,
                reads=["q5"], writes=["q5"])
        self.tt("dve", q3, q1, self.qsc[5][:, 0:NB], ALU.mult,
                reads=["q1", "q5"], writes=["q3"])
        self.cp("dve", self.qsc[6][:, 0:NB].bitcast(I32), q3,
                reads=["q3"], writes=["q6"])
        self.tt("dve", q3, self.qsc[6][:, 0:NB].bitcast(I32), q2, ALU.mult,
                reads=["q6", "q2"], writes=["q3"])
        self.act(q0, q3, AF.Ln, reads=["q3"], writes=["q0"])
        self.ts("dve", q0, q0, -1.0, lngs_col, ALU.mult, ALU.add,
                reads=["q0"] + list(sc_keys), writes=["q0"])
        self.act(q0, q0, AF.Exp, reads=["q0"], writes=["q0"])
        self.ts("dve", q0, q0, 1e30, None, ALU.min, reads=["q0"], writes=["q0"])
        wb = q0.unsqueeze(2).broadcast_to((128, NB, 16))
        s8b = q3.unsqueeze(2).broadcast_to((128, NB, 16))
        t0w, t1w, t2w = t0[:, 0:W], t1[:, 0:W], t2[:, 0:W]
        t0_3 = t0w.rearrange("p (b j) -> p b j", j=16)
        t1_3 = t1w.rearrange("p (b j) -> p b j", j=16)
        self.stt("dve", t0w, src, -1.0, src, ALU.mult, ALU.max,
                 reads=[srck], writes=[t0k])
        self.tt("dve", t0_3, t0_3, wb, ALU.mult, reads=[t0k, "q0"], writes=[t0k])
        self.ts("dve", t0w, t0w, 6.0, None, ALU.min, reads=[t0k], writes=[t0k])
        self.ts("dve", t1w, t0w, 2.0, None, ALU.mult, reads=[t0k], writes=[t1k])
        self.cp("dve", t2w.bitcast(I32), t1w, reads=[t1k], writes=[t2k])
        self.ts("dve", t1w, t2w.bitcast(I32), 0.5, None, ALU.mult,
                reads=[t2k], writes=[t1k])
        self.ts("dve", t2w, t0w, VELT_C, None, ALU.mult, reads=[t0k], writes=[t2k])
        self.tt("dve", t2w, t2w, t0w, ALU.subtract, reads=[t2k, t0k], writes=[t2k])
        self.stt("dve", t2w, t0w, VELT_C, t2w, ALU.mult, ALU.subtract,
                 reads=[t0k, t2k], writes=[t2k])
        self.ts("dve", self.qmask[:, 0:W], t0w, 1.0, None, ALU.is_ge,
                reads=[t0k], writes=["qm"])
        self.sel(t1w, self.qmask[:, 0:W], t2w, t1w,
                 reads=["qm", t2k, t1k], writes=[t1k])
        self.tt("dve", t1_3, t1_3, s8b, ALU.mult, reads=[t1k, "q3"], writes=[t1k])
        self.ts("dve", t2w, src, 0.0, 2.0, ALU.is_ge, ALU.mult,
                reads=[srck], writes=[t2k])
        self.tt("dve", t2w, t1w, t2w, ALU.mult, reads=[t1k, t2k], writes=[t2k])
        self.tt("dve", out_bf, t2w, t1w, ALU.subtract, reads=[t2k, t1k],
                writes=[outk])

    def wquant_phase(self):
        shards = (("gate", self.gate_sh, 4), ("up", self.up_sh, 4),
                  ("down", self.down_sh, 4))
        self.memset("dve", self.col1[:, 0:4], 0.0, writes=["col1"])
        for gi, (wname, srcd, nch) in enumerate(shards):
            def chunk_ap(c, wname=wname, srcd=srcd):
                if wname == "down":
                    return srcd[:, c * D:(c + 1) * D]
                return srcd[c * 128:(c + 1) * 128, :]
            for c in range(nch):
                stg = self.scr[6 + (c % 2)]
                sk = f"s{6 + (c % 2)}"
                self.dma("qs", stg, chunk_ap(c), writes=[sk])
                self.red("dve", self.col1[:, 4:5], stg, ALU.max, absval=True,
                         reads=[sk], writes=["col1"])
                self.tt("dve", self.col1[:, gi:gi + 1], self.col1[:, gi:gi + 1],
                        self.col1[:, 4:5], ALU.max, reads=["col1"], writes=["col1"])
        self.tr(self.pb[0][0:4, 0:128], self.col1[:, 0:4], False,
                reads=["col1"], writes=["p0"])
        self.red("dve", self.scr[0][0:4, 0:1], self.pb[0][0:4, 0:128], ALU.max,
                 reads=["p0"], writes=["s0"])
        self.dma("qs", self.ar_in, self.scr[0][0:4, 0:1], reads=["s0"],
                 writes=["arin"])
        self.sb.op("cc", lambda e: e.collective_compute(
            "AllReduce", ALU.max, replica_groups=[list(range(NCORE))],
            ins=[self.ar_in.opt()], outs=[self.ar_out.opt()]),
            reads=["arin"], writes=["arout"])
        for i in range(3):
            self.dma("qs", self.gw6col[:, i:i + 1],
                     self.ar_out[i:i + 1, 0:1].broadcast_to((128, 1)),
                     reads=["arout"], writes=["gw6"])
        self.act(self.lnwcol, self.gw6col, AF.Ln, reads=["gw6"], writes=["lnw"])
        self.ts("dve", self.gw6col, self.lnwcol, -1.0, LN448, ALU.mult, ALU.add,
                reads=["lnw"], writes=["gw6"])
        self.act(self.gw6col, self.gw6col, AF.Exp, reads=["gw6"], writes=["gw6"])
        self.ts("dve", self.lnwcol, self.lnwcol, -1.0,
                float(LN448 + np.log(np.float64(6.0))), ALU.mult, ALU.add,
                reads=["lnw"], writes=["lnw"])
        for i in range(3):
            self.ts("dve", self.invcol[:, i:i + 1], self.lnwcol[:, i:i + 1], 1.0,
                    self.lnscol[:, (0 if i < 2 else 1):(1 if i < 2 else 2)],
                    ALU.mult, ALU.add, reads=["lnw", "lns"], writes=["inv"])
        self.ts("dve", self.invcol, self.invcol, -1.0, None, ALU.mult,
                reads=["inv"], writes=["inv"])
        self.act(self.invcol, self.invcol, AF.Exp, reads=["inv"], writes=["inv"])
        for gi, (wname, srcd, nch) in enumerate(shards):
            def chunk_ap(c, wname=wname, srcd=srcd):
                if wname == "down":
                    return srcd[:, c * D:(c + 1) * D]
                return srcd[c * 128:(c + 1) * 128, :]
            agv = self.agw_in[wname].rearrange("(g p) r -> p g r", p=128)
            for c in range(nch):
                stg = self.scr[6 + (c % 2)]
                sk = f"s{6 + (c % 2)}"
                self.dma("qs", stg, chunk_ap(c), writes=[sk])
                ob = self.scrb[0]
                self.quant_tile(stg, sk, ob, "b0", D, self.gw6col[:, gi:gi + 1],
                                self.lnwcol[:, gi:gi + 1], ["gw6", "lnw"],
                                self.scr[0], "s0", self.scr[1], "s1",
                                self.scr[2], "s2")
                for h in range(8):
                    self.tr(self.pbt[:, h * 128:(h + 1) * 128],
                            ob[:, h * 128:(h + 1) * 128], True,
                            reads=["b0"], writes=["pt"])
                self.cp("act", self.scrb[1], self.pbt, reads=["pt"], writes=["b1"])
                if wname == "down":
                    outap = agv[:, 8 * c:8 * c + 8, 0:128]
                else:
                    outap = agv[:, 0:8, c * 128:(c + 1) * 128]
                self.dma("qs", outap,
                         self.scrb[1].rearrange("p (hh q) -> p hh q", q=128),
                         reads=["b1"], writes=[f"agwi_{wname}"])
        for wname in ("gate", "up", "down"):
            self.sb.op("cc", lambda e, w=wname: e.collective_compute(
                "AllGather", ALU.bypass, replica_groups=[list(range(NCORE))],
                ins=[self.agw_in[w].opt()], outs=[self.agw_out[w].opt()]),
                reads=[f"agwi_{wname}"], writes=[f"agwo_{wname}"])
        for tp, w in (("wqg", "gate"), ("wqu", "up"), ("wqd", "down")):
            if tp in self.taps:
                self.dma("qs", self.taps[tp].rearrange("i (g r) -> g i r", g=8),
                         self.agw_out[w], reads=[f"agwo_{w}"], writes=["tapw"])

    def linear_tile(self, psums, pskeys, xchunks, wsb, wkey, xkey, bias_rows, bkey,
                    n_out=D, kchunks=8):
        for hh in range(n_out // 512):
            mms = []
            for c in range(kchunks):
                mms.append((psums[hh], xchunks[c],
                            wsb[:, c * n_out + hh * 512: c * n_out + hh * 512 + 512],
                            c == 0, False))
            if bias_rows is not None:
                mms.append((psums[hh], self.ones_row[0:2, 0:128],
                            bias_rows[0:2, hh * 512:hh * 512 + 512], False, True))
            else:
                o, l, r, st_, _ = mms[-1]
                mms[-1] = (o, l, r, st_, True)
            rk = [wkey, xkey, "ones"] + ([bkey] if bias_rows is not None else [])
            self.mmg(mms, reads=rk, writes=[pskeys[hh]])

    def cumsum_tile(self, psums, pskeys, hi_bf, lo_bf, hik, lok, car_hi, car_lo,
                    ckey):
        for hh in range(2):
            sl = slice(hh * 512, hh * 512 + 512)
            mms = [(psums[hh], self.c_triuB, hi_bf[:, sl], True, False),
                   (psums[hh], self.c_triuB, lo_bf[:, sl], False, False),
                   (psums[hh], self.ones_row[0:1, 0:128], car_hi[0:1, sl],
                    False, False),
                   (psums[hh], self.ones_row[0:1, 0:128], car_lo[0:1, sl],
                    False, True)]
            self.mmg(mms, reads=[hik, lok, ckey, "consts", "ones"],
                     writes=[pskeys[hh]])

    def rowreduce(self, pick_col, hi_bf, lo_bf, hik, lok, out_hi, out_lo, okey,
                  add_old=False):
        for hh in range(2):
            sl = slice(hh * 512, hh * 512 + 512)
            mms = [(self.pbc[0:1, 0:512], pick_col, hi_bf[:, sl], True, False),
                   (self.pbc[0:1, 0:512], pick_col, lo_bf[:, sl], False,
                    not add_old)]
            if add_old:
                mms.append((self.pbc[0:1, 0:512], self.ones_row[0:1, 0:1],
                            out_hi[0:1, sl], False, False))
                mms.append((self.pbc[0:1, 0:512], self.ones_row[0:1, 0:1],
                            out_lo[0:1, sl], False, True))
            self.mmg(mms, reads=[hik, lok, okey, "consts", "ones"], writes=["pbc"])
            self.cp("dve", out_hi[0:1, sl], self.pbc[0:1, 0:512], reads=["pbc"],
                    writes=[okey])
            self.tt("dve", out_lo[0:1, sl], self.pbc[0:1, 0:512], out_hi[0:1, sl],
                    ALU.subtract, reads=["pbc", okey], writes=[okey])

    def shift_psum(self, psums, pskeys, hi_bf, lo_bf, hik, lok, prow_hi, prow_lo,
                   pkey):
        for hh in range(2):
            sl = slice(hh * 512, hh * 512 + 512)
            mms = [(psums[hh], self.c_shdB, hi_bf[:, sl], True, False),
                   (psums[hh], self.c_shdB, lo_bf[:, sl], False, False),
                   (psums[hh], self.e0row, prow_hi[0:1, sl], False, False),
                   (psums[hh], self.e0row, prow_lo[0:1, sl], False, True)]
            self.mmg(mms, reads=[hik, lok, pkey, "consts"], writes=[pskeys[hh]])

    def transpose_tile(self, dst_bf, dstk, src_bf, srck, nch=8, dst_ap_list=None):
        for h in range(nch):
            self.tr(self.pbt[:, (h % 8) * 128:(h % 8) * 128 + 128],
                    src_bf[:, h * 128:(h + 1) * 128], True,
                    reads=[srck], writes=["pt"])
            if h % 8 == 7 or h == nch - 1:
                base = (h // 8) * 8
                n = h % 8 + 1
                if dst_ap_list is None:
                    self.cp("act", dst_bf[:, base * 128:(base + n) * 128]
                            if False else dst_bf[:, base * 128:(base + n) * 128],
                            self.pbt[:, 0:n * 128], reads=["pt"], writes=[dstk])

    def transpose_to(self, dst_ap, dstk, src_bf, srck, nch=8):
        for h in range(nch):
            self.tr(self.pbt[:, h * 128:(h + 1) * 128],
                    src_bf[:, h * 128:(h + 1) * 128], True,
                    reads=[srck], writes=["pt"])
        self.cp("act", dst_ap, self.pbt[:, 0:nch * 128]
                .rearrange("p (c q) -> p c q", q=128), reads=["pt"], writes=[dstk])

    def l0_attn_phase(self):
        self.xqT = self.abf.carve("xqT", NT * 1024)
        self.xnR = self.abf.carve("xnR", 8 * 129)
        self.w0 = {}
        for n, srcd in (("Wq1T", self.Wq1T), ("Wk1T", self.Wk1T),
                        ("Wk2T", self.Wk2T), ("WprojT", self.WprojT)):
            self.w0[n] = self.abf.carve(n, 8 * D)
            self.load_w(srcd, self.w0[n], n, D, D)
        xnRv = self.xnR.rearrange("p (c q) -> p c q", q=129)
        xch = [self.xnR[:, c * 129 + 1: c * 129 + 129] for c in range(8)]
        xch_sh = [self.xnR[:, c * 129: c * 129 + 128] for c in range(8)]
        ntiles = min(NT, NT_LIM)
        self.memset("dve", self.nc_carry_dummy, 0.0) if False else None
        for r, k in ((self.carK_hi, "carK"), (self.carK_lo, "carK"),
                     (self.carKV_hi, "carKV"), (self.carKV_lo, "carKV"),
                     (self.carS2_hi, "carS2"), (self.carS2_lo, "carS2"),
                     (self.prow_oar_hi, "proar"), (self.prow_oar_lo, "proar"),
                     (self.prow_Q_hi, "prQ"), (self.prow_Q_lo, "prQ")):
            self.memset("dve", r, 0.0, writes=[k])
        for t in range(ntiles):
            rt = slice(128 * t, 128 * (t + 1))
            self.dma("qs", self.xtile, self.x_in[rt, :], writes=["xt"])
            mcol = self.ln_m[:, t:t + 1]
            rcol = self.ln_r[:, t:t + 1]
            self.red("dve", self.col1[:, 4:5], self.xtile, ALU.add,
                     reads=["xt"], writes=["col1"])
            self.ts("dve", mcol, self.col1[:, 4:5], float(1.0 / D), None, ALU.mult,
                    reads=["col1"], writes=["lnm"])
            xc = self.scr[0]
            self.ts("dve", xc, self.xtile, mcol, None, ALU.subtract,
                    reads=["xt", "lnm"], writes=["s0"])
            self.act(self.scr[1], xc, AF.Square, reads=["s0"], writes=["s1"])
            self.red("dve", self.col1[:, 5:6], self.scr[1], ALU.add,
                     reads=["s1"], writes=["col1"])
            self.act(self.col1[:, 5:6], self.col1[:, 5:6], AF.Ln, bias=1e-5,
                     scale=float(1.0 / D), reads=["col1"], writes=["col1"])
            self.act(rcol, self.col1[:, 5:6], AF.Exp, scale=-0.5,
                     reads=["col1"], writes=["lnr"])
            xn = self.scr[0]
            self.ts("dve", xn, xc, rcol, None, ALU.mult,
                    reads=["s0", "lnr"], writes=["s0"])
            if "xtile" in self.taps:
                self.dma("qa", self.taps["xtile"][rt, :], self.xtile,
                         reads=["xt"], writes=["tapxt"])
            if "xn0" in self.taps:
                self.dma("qs", self.taps["xn0"][rt, :], xn, reads=["s0"],
                         writes=["tapxn0"])
            if "lnm" in self.taps and t == NT_LIM - 1:
                self.dma("qs", self.taps["lnm"], self.ln_m, reads=["lnm"],
                         writes=["taplnm"])
                self.dma("qs", self.taps["lnr"], self.ln_r, reads=["lnr"],
                         writes=["taplnr"])
            if "lncol" in self.taps and t == NT_LIM - 1:
                self.dma("qs", self.taps["lncol"], self.col1, reads=["col1"],
                         writes=["taplncol"])
            if t == 0:
                self.memset("dve", xnRv[:, :, 0:1], 0.0, writes=["xnR"])
            else:
                self.cp("dve", xnRv[:, :, 0:1], xnRv[:, :, 128:129],
                        reads=["xnR"], writes=["xnR"])
            self.cp("act", self.scrb[0], xn, reads=["s0"], writes=["b0"])
            self.transpose_to(xnRv[:, :, 1:129], "xnR", self.scrb[0], "b0")
            self.linear_tile([self.pb[0], self.pb[1]], ["p0", "p1"], xch,
                             self.w0["Wk1T"], "Wk1T", "xnR",
                             self.bias_rows["bk1"], "bsl1")
            self.linear_tile([self.pb[2], self.pb[3]], ["p2", "p3"], xch,
                             self.w0["Wq1T"], "Wq1T", "xnR",
                             self.bias_rows["bq1"], "bsl0")
            self.linear_tile([self.pb[4], self.pb[5]], ["p4", "p5"], xch_sh,
                             self.w0["Wk2T"], "Wk2T", "xnR",
                             self.bias_rows["bk2"], "bsl2")
            K = self.scr[1]
            for hh in range(2):
                self.act(K[:, hh * 512:hh * 512 + 512], self.pb[hh], AF.Exp,
                         reads=[f"p{hh}"], writes=["s1"])
            Q = self.scr[2]
            for hh in range(2):
                sl = slice(hh * 512, hh * 512 + 512)
                self.sigmoid(Q[:, sl], "s2", self.pb[2 + hh], f"p{2 + hh}",
                             self.scr[5][:, sl], "s5", self.scr[6][:, sl], "s6")
            K2 = self.scr[3]
            for hh in range(2):
                sl = slice(hh * 512, hh * 512 + 512)
                self.sigmoid(K2[:, sl], "s3", self.pb[4 + hh], f"p{4 + hh}",
                             self.scr[5][:, sl], "s5", self.scr[6][:, sl], "s6",
                             scale=CS_K2)
            self.hilo("dve", K, self.scrb[1], self.scrb[2], "s1", "b1")
            self.cumsum_tile([self.pb[0], self.pb[1]], ["p0", "p1"],
                             self.scrb[1], self.scrb[2], "b1", "b1",
                             self.carK_hi, self.carK_lo, "carK")
            self.rowreduce(self.c_pick[:, 0:1], self.scrb[1], self.scrb[2],
                           "b1", "b1", self.carK_hi, self.carK_lo, "carK",
                           add_old=True)
            rcp = self.scr[4]
            for hh in range(2):
                sl = slice(hh * 512, hh * 512 + 512)
                self.act(rcp[:, sl], self.pb[hh], AF.Ln, bias=1e-6,
                         reads=[f"p{hh}"], writes=["s4"])
                self.act(rcp[:, sl], rcp[:, sl], AF.Exp, scale=-1.0,
                         reads=["s4"], writes=["s4"])
            KV = self.scr[5]
            self.tt("dve", KV, K, xn, ALU.mult, reads=["s1", "s0"], writes=["s5"])
            self.hilo("dve", KV, self.scrb[1], self.scrb[2], "s5", "b1")
            self.cumsum_tile([self.pb[2], self.pb[3]], ["p2", "p3"],
                             self.scrb[1], self.scrb[2], "b1", "b1",
                             self.carKV_hi, self.carKV_lo, "carKV")
            self.rowreduce(self.c_pick[:, 0:1], self.scrb[1], self.scrb[2],
                           "b1", "b1", self.carKV_hi, self.carKV_lo, "carKV",
                           add_old=True)
            S = self.scr[5]
            for hh in range(2):
                sl = slice(hh * 512, hh * 512 + 512)
                self.tt("dve", S[:, sl], self.pb[2 + hh], rcp[:, sl], ALU.mult,
                        reads=[f"p{2 + hh}", "s4"], writes=["s5"])
            if "S" in self.taps:
                self.dma("qs", self.taps["S"][rt, :], S, reads=["s5"], writes=["tapS"])
            Oar = self.scr[6]
            self.tt("dve", Oar, S, Q, ALU.mult, reads=["s5", "s2"], writes=["s6"])
            self.hilo("dve", Oar, self.scrb[1], self.scrb[2], "s6", "b1")
            self.shift_psum([self.pb[0], self.pb[1]], ["p0", "p1"],
                            self.scrb[1], self.scrb[2], "b1", "b1",
                            self.prow_oar_hi, self.prow_oar_lo, "proar")
            self.rowreduce(self.c_pick[:, 1:2], self.scrb[1], self.scrb[2],
                           "b1", "b1", self.prow_oar_hi, self.prow_oar_lo, "proar")
            E = self.scr[7]
            for hh in range(2):
                sl = slice(hh * 512, hh * 512 + 512)
                self.stt("dve", E[:, sl], self.pb[hh], -1.0, xn[:, sl],
                         ALU.mult, ALU.add, reads=[f"p{hh}", "s0"], writes=["s7"])
            if t == 0:
                self.memset("dve", E[0:1, :], 0.0, writes=["s7"])
            if "E" in self.taps:
                self.dma("qs", self.taps["E"][rt, :], E, reads=["s7"],
                         writes=["tapE"])
            K2E = self.scr[7]
            self.tt("dve", K2E, K2, E, ALU.mult, reads=["s3", "s7"], writes=["s7"])
            self.hilo("dve", K2E, self.scrb[1], self.scrb[2], "s7", "b1")
            self.cumsum_tile([self.pb[4], self.pb[5]], ["p4", "p5"],
                             self.scrb[1], self.scrb[2], "b1", "b1",
                             self.carS2_hi, self.carS2_lo, "carS2")
            self.rowreduce(self.c_pick[:, 0:1], self.scrb[1], self.scrb[2],
                           "b1", "b1", self.carS2_hi, self.carS2_lo, "carS2",
                           add_old=True)
            if "S2" in self.taps:
                for hh in range(2):
                    self.cp("dve", self.scr[4][:, hh * 512:hh * 512 + 512],
                            self.pb[4 + hh], reads=[f"p{4 + hh}"], writes=["s4"])
                self.dma("qs", self.taps["S2"][rt, :], self.scr[4], reads=["s4"],
                         writes=["tapS2"])
            self.hilo("dve", Q, self.scrb[1], self.scrb[2], "s2", "b1")
            self.shift_psum([self.pb[2], self.pb[3]], ["p2", "p3"],
                            self.scrb[1], self.scrb[2], "b1", "b1",
                            self.prow_Q_hi, self.prow_Q_lo, "prQ")
            self.rowreduce(self.c_pick[:, 1:2], self.scrb[1], self.scrb[2],
                           "b1", "b1", self.prow_Q_hi, self.prow_Q_lo, "prQ")
            pre = self.scr[3]
            for hh in range(2):
                sl = slice(hh * 512, hh * 512 + 512)
                self.ts("dve", pre[:, sl], self.pb[2 + hh], CS_K2, None, ALU.mult,
                        reads=[f"p{2 + hh}"], writes=["s3"])
                self.tt("dve", pre[:, sl], pre[:, sl], self.pb[4 + hh], ALU.mult,
                        reads=["s3", f"p{4 + hh}"], writes=["s3"])
            self.tt("dve", self.scrb[0], pre, Oar, ALU.add, reads=["s3", "s6"],
                    writes=["b0"])
            if "Q" in self.taps:
                self.dma("qs", self.taps["Q"][rt, :], Q, reads=["s2"], writes=["tapQ"])
            self.quant_tile(xn, "s0", self.scrb[1], "b1", D, self.gs6col[:, 0:1],
                            self.lnscol[:, 0:1], ["gs6", "lns"],
                            self.scr[4], "s4", self.scr[5], "s5", self.scr[7], "s7")
            self.transpose_to(self.xqT[:, t * 1024:(t + 1) * 1024]
                              .rearrange("p (c q) -> p c q", q=128), "xqT",
                              self.scrb[1], "b1")
            self.transpose_to(self.scrb[3].rearrange("p (c q) -> p c q", q=128),
                              "b3", self.scrb[0], "b0")
            self.linear_tile([self.pb[0], self.pb[1]], ["p0", "p1"],
                             [self.scrb[3][:, c * 128:(c + 1) * 128] for c in range(8)],
                             self.w0["WprojT"], "WprojT", "b3",
                             self.bias_rows["bproj"], "bsl3")
            s0t = self.scr[5]
            for hh in range(2):
                self.cp("dve", s0t[:, hh * 512:hh * 512 + 512], self.pb[hh],
                        reads=[f"p{hh}"], writes=["s5"])
            self.dma("qs", self.s0_dram[rt, :], s0t, reads=["s5"], writes=["s0d"])
            if "s0" in self.taps:
                self.dma("qs", self.taps["s0"][rt, :], s0t, reads=["s5"],
                         writes=["taps0"])
            if "K" in self.taps:
                self.dma("qs", self.taps["K"][rt, :], K, reads=["s1"], writes=["tapK"])
        if "xqT" in self.taps:
            self.dma("qs", self.taps["xqT"].rearrange("(c p) (tt q) -> p tt c q",
                                                      p=128, q=128),
                     self.xqT.rearrange("p (tt c q) -> p tt c q", tt=NT, c=8),
                     reads=["xqT"], writes=["tapxq"])

    def l0_ffn_phase(self):
        self.phase("l0ffn")
        self.abf.free_to("Wq1T")
        fg = [self.abf.carve(f"fg{i}", 8 * 512) for i in range(2)]
        fu = [self.abf.carve(f"fu{i}", 8 * 512) for i in range(2)]
        fd = [self.abf.carve(f"fd{i}", 4 * 1024) for i in range(2)]
        hqT = self.abf.carve("hqT", 512)
        gatev = self.agw_out["gate"]
        upv = self.agw_out["up"]
        downv = [self.agw_out["down"][g].rearrange("(s p) r -> p s r", p=128)
                 for g in range(8)]
        for grp in range(NT // 2):
            tA, tB = 2 * grp, 2 * grp + 1
            for c in range(8):
                par = c % 2
                kg, ku, kd = f"fg{par}", f"fu{par}", f"fd{par}"
                self.dma("qs", fg[par].rearrange("p (h r) -> p h r", r=512),
                         gatev[c].rearrange("(h p) r -> p h r", p=128),
                         reads=["agwo_gate"], writes=[kg])
                self.dma("qs", fu[par].rearrange("p (h r) -> p h r", r=512),
                         upv[c].rearrange("(h p) r -> p h r", p=128),
                         reads=["agwo_up"], writes=[ku])
                for g in range(8):
                    self.dma("qa", fd[par].rearrange("p (s q) -> p s q", q=1024)
                             [:, :, g * 128:(g + 1) * 128],
                             downv[g][:, 4 * c:4 * c + 4, :],
                             reads=["agwo_down"], writes=[kd])
                for ti, tt2 in enumerate((tA, tB)):
                    pg, pu = self.pb[0], self.pb[1]
                    pgk, puk = "p0", "p1"
                    pd = [self.pb[2 + 2 * ti], self.pb[3 + 2 * ti]]
                    pdk = [f"p{2 + 2 * ti}", f"p{3 + 2 * ti}"]
                    xsl = self.xqT[:, tt2 * 1024:(tt2 + 1) * 1024]
                    for ps, psk, wb, wkey in ((pg, pgk, fg[par], kg),
                                              (pu, puk, fu[par], ku)):
                        mms = [(ps, xsl[:, kc * 128:(kc + 1) * 128],
                                wb[:, kc * 512:(kc + 1) * 512], kc == 0, kc == 7)
                               for kc in range(8)]
                        self.mmg(mms, reads=[wkey, "xqT"], writes=[psk])
                    gs_ = self.scr[0][:, 0:512]
                    us_ = self.scr[1][:, 0:512]
                    self.ts("dve", gs_, pg, self.invcol[:, 0:1], None, ALU.mult,
                            reads=[pgk, "inv"], writes=["s0"])
                    self.ts("dve", us_, pu, self.invcol[:, 1:2], None, ALU.mult,
                            reads=[puk, "inv"], writes=["s1"])
                    sg = self.scr[2][:, 0:512]
                    self.sigmoid(sg, "s2", gs_, "s0", self.scr[3][:, 0:512], "s3",
                                 self.scr[4][:, 0:512], "s4")
                    hid = self.scr[5][:, 0:512]
                    self.tt("dve", hid, gs_, sg, ALU.mult, reads=["s0", "s2"],
                            writes=["s5"])
                    self.tt("dve", hid, hid, us_, ALU.mult, reads=["s5", "s1"],
                            writes=["s5"])
                    self.quant_tile(hid, "s5", self.scrb[0][:, 0:512], "b0", 512,
                                    self.gs6col[:, 1:2], self.lnscol[:, 1:2],
                                    ["gs6", "lns"],
                                    self.scr[0], "s0", self.scr[1], "s1",
                                    self.scr[2], "s2")
                    for hh2 in range(4):
                        self.tr(self.pbt[:, hh2 * 128:(hh2 + 1) * 128],
                                self.scrb[0][:, hh2 * 128:(hh2 + 1) * 128], True,
                                reads=["b0"], writes=["pt"])
                    self.cp("act", hqT, self.pbt[:, 0:512], reads=["pt"],
                            writes=["hqT"])
                    for hh2 in range(2):
                        mms = []
                        for sc in range(4):
                            mms.append((pd[hh2], hqT[:, sc * 128:(sc + 1) * 128],
                                        fd[par][:, sc * 1024 + hh2 * 512:
                                                sc * 1024 + hh2 * 512 + 512],
                                        c == 0 and sc == 0, c == 7 and sc == 3))
                        self.mmg(mms, reads=[kd, "hqT"], writes=[pdk[hh2]])
            for ti, tt2 in enumerate((tA, tB)):
                rt = slice(128 * tt2, 128 * (tt2 + 1))
                pd = [self.pb[2 + 2 * ti], self.pb[3 + 2 * ti]]
                pdk = [f"p{2 + 2 * ti}", f"p{3 + 2 * ti}"]
                f0 = self.scr[0]
                for hh in range(2):
                    self.ts("dve", f0[:, hh * 512:hh * 512 + 512], pd[hh],
                            self.invcol[:, 2:3], None, ALU.mult,
                            reads=[pdk[hh], "inv"], writes=["s0"])
                if "f0" in self.taps:
                    self.dma("qs", self.taps["f0"][rt, :], f0, reads=["s0"],
                             writes=["tapf0"])
                sgm = self.scr[1]
                self.sigmoid(sgm, "s1", f0, "s0", self.scr[2], "s2",
                             self.scr[3], "s3")
                g0 = self.scr[2]
                self.tt("dve", g0, f0, sgm, ALU.mult, reads=["s0", "s1"],
                        writes=["s2"])
                self.dma("qs", self.g0_dram[rt, :], g0, reads=["s2"], writes=["g0d"])

    def l0_combine_phase(self):
        for t in range(NT):
            rt = slice(128 * t, 128 * (t + 1))
            self.dma("qs", self.xtile, self.x_in[rt, :], writes=["xt"])
            self.dma("qa", self.scr[0], self.s0_dram[rt, :], reads=["s0d"],
                     writes=["s0"])
            self.dma("qp", self.scr[1], self.g0_dram[rt, :], reads=["g0d"],
                     writes=["s1"])
            self.tt("dve", self.scr[0], self.scr[0], self.scr[1], ALU.mult,
                    reads=["s0", "s1"], writes=["s0"])
            self.tt("dve", self.scr[2], self.xtile, self.scr[0], ALU.add,
                    reads=["xt", "s0"], writes=["s2"])
            self.dma("qs", self.x1_dram[rt, :], self.scr[2], reads=["s2"],
                     writes=["x1d"])
            if "x1" in self.taps:
                self.dma("qs", self.taps["x1"][rt, :], self.scr[2], reads=["s2"],
                         writes=["tapx1"])

    def l1_norm_phase(self):
        self.phase("l1norm")
        self.abf.free_to("xqT")
        self.xn1T = self.abf.carve("xn1T", 8 * T)
        agv = self.ag_xn_in.rearrange("(c p) t -> p c t", p=128)
        for t in range(NT):
            rt = slice(128 * t, 128 * (t + 1))
            self.dma("qs", self.xtile, self.x1_dram[rt, :], reads=["x1d"],
                     writes=["xt"])
            mcol = self.ln_m[:, t:t + 1]
            rcol = self.ln_r[:, t:t + 1]
            self.red("dve", self.col1[:, 4:5], self.xtile, ALU.add,
                     reads=["xt"], writes=["col1"])
            self.ts("dve", mcol, self.col1[:, 4:5], float(1.0 / D), None, ALU.mult,
                    reads=["col1"], writes=["lnm"])
            xc = self.scr[0]
            self.ts("dve", xc, self.xtile, mcol, None, ALU.subtract,
                    reads=["xt", "lnm"], writes=["s0"])
            self.act(self.scr[1], xc, AF.Square, reads=["s0"], writes=["s1"])
            self.red("dve", self.col1[:, 5:6], self.scr[1], ALU.add,
                     reads=["s1"], writes=["col1"])
            self.act(self.col1[:, 5:6], self.col1[:, 5:6], AF.Ln, bias=1e-5,
                     scale=float(1.0 / D), reads=["col1"], writes=["col1"])
            self.act(rcol, self.col1[:, 5:6], AF.Exp, scale=-0.5,
                     reads=["col1"], writes=["lnr"])
            xn = self.scr[0]
            self.ts("dve", xn, xc, rcol, None, ALU.mult,
                    reads=["s0", "lnr"], writes=["s0"])
            self.cp("act", self.scrb[0], xn, reads=["s0"], writes=["b0"])
            self.transpose_tile(self.xn1T[:, t * 1024:(t + 1) * 1024], "xn1T",
                                self.scrb[0], "b0")
            self.dma("qs", agv[:, :, 128 * t:128 * (t + 1)],
                     self.xn1T[:, t * 1024:(t + 1) * 1024]
                     .rearrange("p (c q) -> p c q", q=128),
                     reads=["xn1T"], writes=["agxn"])
        self.sb.op("cc", lambda e: e.collective_compute(
            "AllGather", ALU.bypass, replica_groups=[list(range(NCORE))],
            ins=[self.ag_xn_in.opt()], outs=[self.ag_xn_out.opt()]),
            reads=["agxn"], writes=["agxn_out"])
        for j in range(NCORE):
            for b in range(NCORE):
                self.dma("qp", self.rs_in[j][b].rearrange("(c p) q -> p c q",
                                                          p=128),
                         self.ag_xn_out[b].rearrange("(c p) q -> p c q", p=128)
                         [:, :, j * POS:(j + 1) * POS],
                         reads=["agxn_out"], writes=["rsin"])
        self.sb.op("cc", lambda e: e.collective_compute(
            "ReduceScatter", ALU.max, replica_groups=[list(range(NCORE))],
            ins=[self.rs_in.opt()], outs=[self.rs_out.opt()]),
            reads=["rsin"], writes=["rsout"])
        if "xn1T" in self.taps:
            self.dma("qs", self.taps["xn1T"], self.ag_xn_in, reads=["agxn"],
                     writes=["tapxn1"])

    def l1_attn_phase(self):
        self.phase("l1attn")
        self.abf.free_to("xn1T")
        wqkv = self.abf.carve("wqkv", 8 * CD)
        for c in range(8):
            for j in range(3):
                stg = self.wstage[j % 2]
                sk = f"wstage{j % 2}"
                self.dma("qa", stg, self.WqkvT[c * 128:(c + 1) * 128,
                                               j * 1024:(j + 1) * 1024],
                         writes=[sk])
                self.cp("act", wqkv[:, c * CD + j * 1024: c * CD + (j + 1) * 1024],
                        stg, reads=[sk], writes=["wqkv"])
        wz = self.abf.carve("wz", 8 * D)
        self.load_w(self.WzT, wz, "wz", D, D)
        wout = self.abf.carve("wout", 8 * D)
        self.load_w(self.WoutT, wout, "wout", D, D)
        wab = self.abf.carve("wab", 8 * 2 * H)
        for c in range(8):
            self.dma("qa", self.wstage[0][:, 0:2 * H],
                     self.WabT[c * 128:(c + 1) * 128, :], writes=["wstage0"])
            self.cp("act", wab[:, c * 2 * H:(c + 1) * 2 * H],
                    self.wstage[0][:, 0:2 * H], reads=["wstage0"], writes=["wab"])
        self.dma("qs", self.convw_sb.rearrange("p (m j) -> p m j", j=4),
                 self.convw.rearrange("(m p) j -> p m j", p=128), writes=["cw"])
        self.dma("qs", self.c16[0], self.A_log.broadcast_to((128, H)),
                 writes=["c16a"])
        self.act(self.c16[0], self.c16[0], AF.Exp, reads=["c16a"], writes=["c16a"])
        self.ts("dve", self.c16[0], self.c16[0], -1.0, None, ALU.mult,
                reads=["c16a"], writes=["c16a"])
        ring = self.abf.carve("ring", 24 * 131)
        ringv = ring.rearrange("p (m q) -> p m q", q=131)
        mT = self.abf.carve("mT", 24 * 128)
        x1t = self.abf.carve("x1t", 8 * 128)
        o3T = self.abf.carve("o3T", 8 * 128)
        self.memset("dve", ring, 0.0, writes=["ring"])
        self.memset("dve", self.S_f, 0.0, writes=["Sf"])
        self.memset("dve", self.S_bf, 0.0, writes=["Sbf"])
        self.memset("dve", self.Ltok, 0.0, writes=["Ltok"])
        self.memset("dve", self.Rtok, 0.0, writes=["Rtok"])
        Lv = self.Ltok.rearrange("p (h q) -> p h q", q=64)
        Rv = self.Rtok.rearrange("p (h q) -> p h q", q=64)
        for q in (0, 1, 2, 8, 9):
            self.memset("dve", Lv[:, :, q:q + 1], 1.0, writes=["Ltok"])
        for q in (3, 4, 5, 6, 7):
            self.memset("dve", Rv[:, :, q:q + 1], 1.0, writes=["Rtok"])
        agv = self.ag_xn_in.rearrange("(c p) t -> p c t", p=128)
        L1 = self.l1c
        sl16 = lambda i: L1[:, i * H:(i + 1) * H]
        ntiles = min(NT, NT_LIM)
        for t in range(ntiles):
            rt = slice(128 * t, 128 * (t + 1))
            self.dma("qs", x1t.rearrange("p (c q) -> p c q", q=128),
                     agv[:, :, rt], reads=["agxn"], writes=["x1t"])
            for m in range(24):
                ps, psk = self.pb[m // 4], f"p{m // 4}"
                col = (m % 4) * 128
                mms = [(ps[:, col:col + 128], wqkv[:, c * CD + m * 128:
                                                  c * CD + m * 128 + 128],
                        x1t[:, c * 128:(c + 1) * 128], c == 0, c == 7)
                       for c in range(8)]
                self.mmg(mms, reads=["wqkv", "x1t"], writes=[psk])
            self.cp("dve", ringv[:, :, 0:3], ringv[:, :, 128:131],
                    reads=["ring"], writes=["ring"])
            for m in range(24):
                ps, psk = self.pb[m // 4], f"p{m // 4}"
                col = (m % 4) * 128
                self.cp("act", ringv[:, m, 3:131], ps[:, col:col + 128],
                        reads=[psk], writes=["ring"])
            cwv = self.convw_sb.rearrange("p (m j) -> p m j", j=4)
            for part in range(3):
                msl = slice(part * 8, part * 8 + 8)
                acc = self.scr[0][:, 0:1024].rearrange("p (m q) -> p m q", q=128)
                tmp = self.scr[1][:, 0:1024].rearrange("p (m q) -> p m q", q=128)
                for j in range(4):
                    cwb = cwv[:, msl, j:j + 1].broadcast_to((128, 8, 128))
                    src = ring.rearrange("p (m q) -> p m q", q=131)[:, msl, j:j + 128]
                    if j == 0:
                        self.tt("pool", acc, src, cwb, ALU.mult,
                                reads=["ring", "cw"], writes=["s0"])
                    else:
                        self.tt("pool", tmp, src, cwb, ALU.mult,
                                reads=["ring", "cw"], writes=["s1"])
                        self.tt("pool", acc, acc, tmp, ALU.add,
                                reads=["s0", "s1"], writes=["s0"])
                a2 = self.scr[0]
                self.sigmoid(self.scr[2], "s2", a2, "s0", self.scr[3], "s3",
                             self.scr[4], "s4")
                self.tt("dve", mT[:, part * 1024:(part + 1) * 1024], a2,
                        self.scr[2], ALU.mult, reads=["s0", "s2"], writes=["mT"])
            if "mixed" in self.taps:
                self.dma("qs", self.taps["mixed"]
                         .rearrange("(m r) (tt q) -> tt r m q", r=128, q=128)[t],
                         mT.rearrange("p (m q) -> p m q", q=128),
                         reads=["mT"], writes=["tapmx"])
            for gi, dst in ((0, self.scrb[1]), (1, self.scrb[2]),
                            (2, self.scrb[3])):
                for h8 in range(8):
                    self.tr(self.pbt[:, h8 * 128:(h8 + 1) * 128],
                            mT[:, (gi * 8 + h8) * 128:(gi * 8 + h8 + 1) * 128],
                            True, reads=["mT"], writes=["pt"])
                self.cp("act", dst, self.pbt, reads=["pt"], writes=[f"b{gi + 1}"])
            mms = [(self.pbc[:, 0:32], x1t[:, c * 128:(c + 1) * 128],
                    wab[:, c * 2 * H:(c + 1) * 2 * H], c == 0, False)
                   for c in range(8)]
            mms.append((self.pbc[:, 16:32], self.ones_row[0:2, 0:128],
                        self.dt_rows[0:2, :], False, True))
            self.mmg(mms, reads=["wab", "x1t", "ones", "br_dt"], writes=["pbc"])
            m0 = self.scr[5][:, 0:32]
            m1 = self.scr[5][:, 32:64]
            el = self.scr[5][:, 64:96]
            self.ts("dve", m0, self.pbc[:, 0:32], 0.0, None, ALU.min,
                    reads=["pbc"], writes=["s5"])
            self.ts("dve", m1, self.pbc[:, 0:32], 0.0, None, ALU.max,
                    reads=["pbc"], writes=["s5"])
            self.tt("dve", el, m0, m1, ALU.subtract, reads=["s5"], writes=["s5"])
            self.act(el, el, AF.Exp, reads=["s5"], writes=["s5"])
            self.act(el, el, AF.Ln, bias=1.0, reads=["s5", "cbias"], writes=["s5"])
            lnbeta = self.scr[6][:, 0:16]
            sp = self.scr[6][:, 16:32]
            self.tt("dve", lnbeta, m0[:, 0:16], el[:, 0:16], ALU.subtract,
                    reads=["s5"], writes=["s6"])
            self.tt("dve", sp, m1[:, 16:32], el[:, 16:32], ALU.add,
                    reads=["s5"], writes=["s6"])
            gcol = self.scr[6][:, 32:48]
            self.tt("dve", gcol, sp, self.c16[0], ALU.mult,
                    reads=["s6", "c16a"], writes=["s6"])
            ghi_b = self.scrb[0][:, 0:16]
            glo_b = self.scrb[0][:, 16:32]
            self.cp("dve", ghi_b, gcol, reads=["s6"], writes=["b0"])
            self.tt("dve", glo_b, gcol, ghi_b, ALU.subtract, reads=["s6", "b0"],
                    writes=["b0"])
            if "gdec" in self.taps:
                self.dma("qs", self.taps["gdec"][rt, :], gcol, reads=["s6"],
                         writes=["tapgd"])
            if "beta" in self.taps:
                self.dma("qs", self.taps["beta"][rt, :], lnbeta, reads=["s6"],
                         writes=["tapbe"])
            mms = [(self.pbc[:, 64:80], self.c_triuB, ghi_b, True, False),
                   (self.pbc[:, 64:80], self.c_triuB, glo_b, False, True)]
            self.mmg(mms, reads=["b0", "consts"], writes=["pbc"])
            mms = [(self.pbc[:, 80:96], self.c_ones128, ghi_b, True, False),
                   (self.pbc[:, 80:96], self.c_ones128, glo_b, False, True)]
            self.mmg(mms, reads=["b0", "consts"], writes=["pbc"])
            G = sl16(0)
            self.cp("dve", G, self.pbc[:, 64:80], reads=["pbc"], writes=["l1c"])
            rtmp = sl16(1)
            self.cp("dve", Rv[:, :, 0:1], G.unsqueeze(2), reads=["l1c"],
                    writes=["Rtok"])
            self.tt("dve", rtmp, G, Rv[:, :, 0], ALU.subtract,
                    reads=["l1c", "Rtok"], writes=["l1c"])
            self.cp("dve", Rv[:, :, 1:2], rtmp.unsqueeze(2), reads=["l1c"],
                    writes=["Rtok"])
            self.tt("dve", Rv[:, :, 2], rtmp, Rv[:, :, 1], ALU.subtract,
                    reads=["l1c", "Rtok"], writes=["Rtok"])
            for q in range(3):
                self.ts("dve", Lv[:, :, 3 + q], Rv[:, :, q], -1.0, None,
                        ALU.mult, reads=["Rtok"], writes=["Ltok"])
            sq = self.scr[5]
            self.act(sq, self.scrb[1], AF.Square, reads=["b1"], writes=["s5"])
            self.red("dve", self.c16[1], sq.rearrange("p (h e) -> p h e", e=HD),
                     ALU.add, reads=["s5"], writes=["c16b"])
            lnrq = sl16(3)
            self.act(lnrq, self.c16[1], AF.Ln, bias=1e-6,
                     reads=["c16b", "cbias"], writes=["l1c"])
            self.ts("dve", lnrq, lnrq, -0.5, float(np.log(np.float64(HD ** -0.5))),
                    ALU.mult, ALU.add, reads=["l1c"], writes=["l1c"])
            self.act(sq, self.scrb[2], AF.Square, reads=["b2"], writes=["s5"])
            self.red("dve", self.c16[1], sq.rearrange("p (h e) -> p h e", e=HD),
                     ALU.add, reads=["s5"], writes=["c16b"])
            lnrk = sl16(7)
            self.act(lnrk, self.c16[1], AF.Ln, bias=1e-6,
                     reads=["c16b", "cbias"], writes=["l1c"])
            self.ts("dve", lnrk, lnrk, -0.5, None, ALU.mult,
                    reads=["l1c"], writes=["l1c"])
            lnbk = sl16(2)
            self.tt("dve", lnbk, lnbeta, lnrk, ALU.add, reads=["s6", "l1c"],
                    writes=["l1c"])
            self.cp("dve", Lv[:, :, 6:7], lnbk.unsqueeze(2), reads=["l1c"],
                    writes=["Ltok"])
            self.tt("dve", Lv[:, :, 7], lnbk, Lv[:, :, 6], ALU.subtract,
                    reads=["l1c", "Ltok"], writes=["Ltok"])
            self.cp("dve", Rv[:, :, 8:9], lnrq.unsqueeze(2), reads=["l1c"],
                    writes=["Rtok"])
            self.tt("dve", Rv[:, :, 9], lnrq, Rv[:, :, 8], ALU.subtract,
                    reads=["l1c", "Rtok"], writes=["Rtok"])
            scolq = sl16(4)
            self.tt("dve", scolq, G, lnrq, ALU.add, reads=["l1c"], writes=["l1c"])
            self.act(scolq, scolq, AF.Exp, reads=["l1c"], writes=["l1c"])
            scolk = sl16(5)
            self.tt("dve", scolk, self.pbc[:, 80:96], G, ALU.subtract,
                    reads=["pbc", "l1c"], writes=["l1c"])
            self.tt("dve", scolk, scolk, lnbk, ALU.add, reads=["l1c"],
                    writes=["l1c"])
            self.act(scolk, scolk, AF.Exp, reads=["l1c"], writes=["l1c"])
            expGE = sl16(6)
            self.act(expGE, self.pbc[:, 80:96], AF.Exp, reads=["pbc"],
                     writes=["l1c"])
            for b8 in range(8):
                self.tr(self.pbt[:, b8 * 128:(b8 + 1) * 128],
                        self.Ltok[:, b8 * 128:(b8 + 1) * 128], True,
                        reads=["Ltok"], writes=["pt"])
            self.cp("act", self.Lpk, self.pbt, reads=["pt"], writes=["Lpk"])
            for b8 in range(8):
                self.tr(self.pbt[:, b8 * 128:(b8 + 1) * 128],
                        self.Rtok[:, b8 * 128:(b8 + 1) * 128], True,
                        reads=["Rtok"], writes=["pt"])
            self.cp("act", self.Rpk, self.pbt, reads=["pt"], writes=["Rpk"])
            for grp in range(4):
                for hl in range(4):
                    h = grp * 4 + hl
                    pbase = 64 * (h % 2)
                    blk = h // 2
                    col = hl * 128
                    mms = [(self.pb[0][:, col:col + 128],
                            self.Lpk[pbase:pbase + 64, blk * 128:(blk + 1) * 128],
                            self.Rpk[pbase:pbase + 64, blk * 128:(blk + 1) * 128],
                            True, False),
                           (self.pb[0][:, col:col + 128], self.c_identB,
                            self.c_maskB, False, True)]
                    self.mmg(mms, reads=["Lpk", "Rpk", "consts"], writes=["p0"])
                    kq = (8 + h // 2) * 128
                    qq = (h // 2) * 128
                    mms = [(self.pb[1][:, col:col + 128],
                            mT[pbase:pbase + 64, kq:kq + 128],
                            mT[pbase:pbase + 64, qq:qq + 128], True, True)]
                    self.mmg(mms, reads=["mT"], writes=["p1"])
                expd = self.scr[5][:, 0:512]
                self.act(expd, self.pb[0][:, 0:512], AF.Exp, reads=["p0"],
                         writes=["s5"])
                at = self.scrb[0][:, 0:512]
                self.tt("dve", at, expd, self.pb[1][:, 0:512], ALU.mult,
                        reads=["s5", "p1"], writes=["b0"])
                for hl in range(4):
                    h = grp * 4 + hl
                    pbase = 64 * (h % 2)
                    qq = (h // 2) * 128
                    oi = self.pb[2 + h // 8]
                    ock = self.pb[4 + h // 8]
                    ocol = (h % 8) * 64
                    self.mmg([(oi[:, ocol:ocol + 64], at[:, hl * 128:hl * 128 + 128],
                               self.scrb[3][:, h * 64:(h + 1) * 64], True, True)],
                             reads=["b0", "b3"], writes=[f"p{2 + h // 8}"])
                    scol = (h // 2) * 64
                    self.mmg([(ock[:, ocol:ocol + 64],
                               mT[pbase:pbase + 64, qq:qq + 128],
                               self.S_bf[pbase:pbase + 64, scol:scol + 64],
                               True, True)],
                             reads=["mT", "Sbf"], writes=[f"p{4 + h // 8}"])
            obuf = self.scr[4]
            o3 = obuf.rearrange("p (h e) -> p h e", e=HD)
            for hh in range(2):
                pi, pc = self.pb[2 + hh], self.pb[4 + hh]
                sl = slice(hh * 512, hh * 512 + 512)
                scb = scolq[:, hh * 8:(hh + 1) * 8].unsqueeze(2) \
                    .broadcast_to((128, 8, 64))
                self.tt("dve", o3[:, hh * 8:(hh + 1) * 8, :],
                        pc.rearrange("p (h e) -> p h e", e=HD), scb, ALU.mult,
                        reads=[f"p{4 + hh}", "l1c"], writes=["s4"])
                self.tt("dve", obuf[:, sl], obuf[:, sl], pi, ALU.add,
                        reads=["s4", f"p{2 + hh}"], writes=["s4"])
            if "o_raw" in self.taps:
                self.dma("qs", self.taps["o_raw"][rt, :], obuf, reads=["s4"],
                         writes=["tapor"])
            ssc = self.scrb[1][:, 0:512]
            for par in range(2):
                pr = slice(64 * par, 64 * par + 64)
                self.tt("dve", ssc[pr, :].rearrange("p (j e) -> p j e", e=HD),
                        self.S_f[pr, :].rearrange("p (j e) -> p j e", e=HD),
                        expGE[pr, par::2].unsqueeze(2)
                        .broadcast_to((64, 8, HD)), ALU.mult,
                        reads=["Sf", "l1c"], writes=["b1"])
            kp = self.scrb[0]
            self.tt("dve", kp.rearrange("p (h e) -> p h e", e=HD),
                    self.scrb[2].rearrange("p (h e) -> p h e", e=HD),
                    scolk.unsqueeze(2).broadcast_to((128, H, HD)), ALU.mult,
                    reads=["b2", "l1c"], writes=["b0"])
            for h in range(H):
                par = h % 2
                pr = slice(64 * par, 64 * par + 64)
                ps, psk = self.pb[par], f"p{par}"
                scol = (h // 2) * 64
                mms = [(ps[pr, scol:scol + 64], kp[:, h * 64:(h + 1) * 64],
                        self.scrb[3][:, h * 64:(h + 1) * 64], True, False),
                       (ps[pr, scol:scol + 64], self.c_identB[pr, pr],
                        ssc[pr, scol:scol + 64], False, True)]
                self.mmg(mms, reads=["b0", "b3", "b1", "consts"], writes=[psk])
            for par in range(2):
                pr = slice(64 * par, 64 * par + 64)
                self.cp("act", self.S_f[pr, :], self.pb[par][pr, 0:512],
                        reads=[f"p{par}"], writes=["Sf"])
                self.cp("dve", self.S_bf[pr, :], self.S_f[pr, :],
                        reads=["Sf"], writes=["Sbf"])
            for hh in range(2):
                mms = [(self.pb[2 + hh], x1t[:, c * 128:(c + 1) * 128],
                        wz[:, c * D + hh * 512: c * D + hh * 512 + 512],
                        c == 0, c == 7) for c in range(8)]
                self.mmg(mms, reads=["wz", "x1t"], writes=[f"p{2 + hh}"])
            zf = self.scr[0]
            for hh in range(2):
                sl = slice(hh * 512, hh * 512 + 512)
                self.cp("dve", zf[:, sl], self.pb[2 + hh], reads=[f"p{2 + hh}"],
                        writes=["s0"])
            self.sigmoid(self.scr[1], "s1", zf, "s0", self.scr[2], "s2",
                         self.scr[3], "s3")
            sz = self.scr[1]
            self.tt("dve", sz, zf, sz, ALU.mult, reads=["s0", "s1"], writes=["s1"])
            sqo = self.scr[2]
            self.act(sqo, obuf, AF.Square, reads=["s4"], writes=["s2"])
            var = self.c16[1]
            self.red("dve", var, sqo.rearrange("p (h e) -> p h e", e=HD),
                     ALU.add, reads=["s2"], writes=["c16b"])
            rs = sl16(1)
            self.act(rs, var, AF.Ln, bias=1e-6, scale=float(1.0 / HD),
                     reads=["c16b", "cbias"], writes=["l1c"])
            self.act(rs, rs, AF.Exp, scale=-0.5, reads=["l1c"], writes=["l1c"])
            o2 = self.scr[3]
            self.tt("dve", o2.rearrange("p (h e) -> p h e", e=HD), o3,
                    rs.unsqueeze(2).broadcast_to((128, H, HD)), ALU.mult,
                    reads=["s4", "l1c"], writes=["s3"])
            o2b = self.scrb[0]
            self.tt("dve", o2b, o2, sz, ALU.mult, reads=["s3", "s1"],
                    writes=["b0"])
            for h8 in range(8):
                self.tr(self.pbt[:, h8 * 128:(h8 + 1) * 128],
                        o2b[:, h8 * 128:(h8 + 1) * 128], True,
                        reads=["b0"], writes=["pt"])
            self.cp("act", o3T, self.pbt, reads=["pt"], writes=["o3T"])
            for hh in range(2):
                mms = [(self.pb[hh + 2], o3T[:, c * 128:(c + 1) * 128],
                        wout[:, c * D + hh * 512: c * D + hh * 512 + 512],
                        c == 0, c == 7) for c in range(8)]
                self.mmg(mms, reads=["wout", "o3T"], writes=[f"p{hh + 2}"])
            s1t = self.scr[2]
            for hh in range(2):
                sl = slice(hh * 512, hh * 512 + 512)
                self.cp("dve", s1t[:, sl], self.pb[hh + 2], reads=[f"p{hh + 2}"],
                        writes=["s2"])
            self.dma("qs", self.s1_dram[rt, :], s1t, reads=["s2"], writes=["s1d"])
            if "s1" in self.taps:
                self.dma("qs", self.taps["s1"][rt, :], s1t, reads=["s2"],
                         writes=["taps1"])
        self.abf.free_to("wqkv")

    def l1_ffn_phase(self):
        self.phase("l1ffn")
        wsb = {}
        for n in ("wf", "wi", "wc", "uf", "ui", "uc"):
            wsb[n] = self.abf.carve("w_" + n, 8 * D)
            self.load_w(getattr(self, n + "T"), wsb[n], "w_" + n, D, D)
        for i, n in enumerate(("bf", "bi", "bc", "bo")):
            srcd = getattr(self, n + "_in")
            self.dma("qs", self.srow, srcd, reads=["srow"], writes=["srow"])
            self.cp("dve", self.scrb[0][0:1, :], self.srow, reads=["srow"],
                    writes=["b0"])
            self.tt("dve", self.scrb[1][0:1, :], self.srow, self.scrb[0][0:1, :],
                    ALU.subtract, reads=["srow", "b0"], writes=["b1"])
            self.dma("qs", self.bias_rows[n][0:1, :], self.scrb[0][0:1, :],
                     reads=["b0"], writes=[f"bsl{i}"])
            self.dma("qs", self.bias_rows[n][1:2, :], self.scrb[1][0:1, :],
                     reads=["b1"], writes=[f"bsl{i}"])
        xslb = self.abf.carve("xslb", 8 * 128)
        ctbb = self.abf.carve("ctbb", 8 * 128)
        self.dma("qs", self.scr[6], self.in_c_in.broadcast_to((128, D)),
                 writes=["s6"])
        self.dma("qa", self.scr[7], self.in_c_in.broadcast_to((128, D)),
                 writes=["s7"])
        for step in range(NCORE):
            for blk in range(2):
                cbuf, ckey = (self.scr[6], "s6") if blk == 0 else (self.scr[7], "s7")
                self.dma("qs", xslb.rearrange("p (c q) -> p c q", q=128),
                         self.rs_out[step].rearrange("(c p) q -> p c q", p=128)
                         [:, :, blk * 128:blk * 128 + 128],
                         reads=["rsout"], writes=["xsl"])
                self.cp("dve", self.scrb[2], cbuf, reads=[ckey], writes=["b2"])
                for h8 in range(8):
                    self.tr(self.pbt[:, h8 * 128:(h8 + 1) * 128],
                            self.scrb[2][:, h8 * 128:(h8 + 1) * 128], True,
                            reads=["b2"], writes=["pt"])
                self.cp("act", ctbb, self.pbt, reads=["pt"], writes=["ctb"])
                for gi, (n, un, bn) in enumerate(
                        (("wf", "uf", "bf"), ("wi", "ui", "bi"),
                         ("wc", "uc", "bc"))):
                    for hh in range(2):
                        ps, psk = self.pb[gi * 2 + hh], f"p{gi * 2 + hh}"
                        mms = [(ps, xslb[:, c * 128:(c + 1) * 128],
                                wsb[n][:, c * D + hh * 512: c * D + hh * 512 + 512],
                                c == 0, False) for c in range(8)]
                        mms += [(ps, ctbb[:, c * 128:(c + 1) * 128],
                                 wsb[un][:, c * D + hh * 512:
                                         c * D + hh * 512 + 512],
                                 False, False) for c in range(8)]
                        bslot = f"bsl{('bf', 'bi', 'bc').index(bn)}"
                        mms.append((ps, self.ones_row[0:2, 0:128],
                                    self.bias_rows[bn][0:2, hh * 512:hh * 512 + 512],
                                    False, True))
                        self.mmg(mms, reads=["w_" + n, "w_" + un, "xsl", "ctb",
                                             "ones", bslot], writes=[psk])
                fcol = self.scr[0]
                for hh in range(2):
                    self.cp("dve", fcol[:, hh * 512:hh * 512 + 512], self.pb[hh],
                            reads=[f"p{hh}"], writes=["s0"])
                self.sigmoid(fcol, "s0", fcol, "s0", self.scr[3], "s3",
                             self.scr[4], "s4")
                icol = self.scr[1]
                for hh in range(2):
                    self.cp("dve", icol[:, hh * 512:hh * 512 + 512],
                            self.pb[2 + hh], reads=[f"p{2 + hh}"], writes=["s1"])
                self.sigmoid(icol, "s1", icol, "s1", self.scr[3], "s3",
                             self.scr[4], "s4")
                gcol = self.scr[2]
                for hh in range(2):
                    self.cp("dve", gcol[:, hh * 512:hh * 512 + 512],
                            self.pb[4 + hh], reads=[f"p{4 + hh}"], writes=["s2"])
                self.sigmoid(self.scr[3], "s3", gcol, "s2", self.scr[4], "s4",
                             self.scr[5], "s5", scale=2.0)
                self.ts("dve", gcol, self.scr[3], 2.0, -1.0, ALU.mult, ALU.add,
                        reads=["s3"], writes=["s2"])
                self.tt("dve", cbuf, fcol, cbuf, ALU.mult,
                        reads=["s0", ckey], writes=[ckey])
                self.tt("dve", self.scr[3], icol, gcol, ALU.mult,
                        reads=["s1", "s2"], writes=["s3"])
                self.tt("dve", cbuf, cbuf, self.scr[3], ALU.add,
                        reads=[ckey, "s3"], writes=[ckey])
        if "cfin" in self.taps:
            self.dma("qs", self.taps["cfin"][0:128, :], self.scr[6],
                     reads=["s6"], writes=["tapcf"])
            self.dma("qs", self.taps["cfin"][128:256, :], self.scr[7],
                     reads=["s7"], writes=["tapcf"])
        self.abf.free_to("w_wf")
        ctb2 = self.abf.carve("ctb2", 8 * POS)
        wuo = self.abf.carve("w_uo", 8 * D)
        self.load_w(self.uoT, wuo, "w_uo", D, D)
        for blk, csrc, ck in ((0, self.scr[6], "s6"), (1, self.scr[7], "s7")):
            self.cp("dve", self.scrb[2], csrc, reads=[ck], writes=["b2"])
            for h8 in range(8):
                self.tr(self.pbt[:, h8 * 128:(h8 + 1) * 128],
                        self.scrb[2][:, h8 * 128:(h8 + 1) * 128], True,
                        reads=["b2"], writes=["pt"])
            self.cp("act", ctb2.rearrange("p (c q) -> p c q", q=POS)
                    [:, :, blk * 128:blk * 128 + 128],
                    self.pbt.rearrange("p (c q) -> p c q", q=128),
                    reads=["pt"], writes=["ctb"])
        for blk in range(2):
            for hh in range(2):
                ps, psk = self.pb[blk * 2 + hh], f"p{blk * 2 + hh}"
                mms = [(ps, ctb2[:, c * POS + blk * 128: c * POS + blk * 128 + 128],
                        wuo[:, c * D + hh * 512: c * D + hh * 512 + 512],
                        c == 0, c == 7) for c in range(8)]
                self.mmg(mms, reads=["w_uo", "ctb"], writes=[psk])
        cuo = self.scr[0]
        for blk in range(2):
            for hh in range(2):
                self.cp("dve", cuo[:, hh * 512:hh * 512 + 512],
                        self.pb[blk * 2 + hh], reads=[f"p{blk * 2 + hh}"],
                        writes=["s0"])
            self.dma("qs", self.ag_cuo_in[blk * 128:blk * 128 + 128, :], cuo,
                     reads=["s0"], writes=["cuoin"])
        self.sb.op("cc", lambda e: e.collective_compute(
            "AllGather", ALU.bypass, replica_groups=[list(range(NCORE))],
            ins=[self.ag_cuo_in.opt()], outs=[self.ag_cuo_out.opt()]),
            reads=["cuoin"], writes=["cuoout"])

    def l1_final_phase(self):
        self.phase("l1fin")
        self.abf.free_to("ctb2")
        wwo = self.abf.carve("w_wo", 8 * D)
        self.load_w(self.woT, wwo, "w_wo", D, D)
        x1t = self.abf.carve("fx1t", 8 * 128)
        agv = self.ag_xn_in.rearrange("(c p) t -> p c t", p=128)
        ntiles = min(NT, NT_LIM)
        for t in range(ntiles):
            rt = slice(128 * t, 128 * (t + 1))
            self.dma("qs", x1t.rearrange("p (c q) -> p c q", q=128),
                     agv[:, :, rt], reads=["agxn"], writes=["fx1t"])
            for hh in range(2):
                mms = [(self.pb[hh], x1t[:, c * 128:(c + 1) * 128],
                        wwo[:, c * D + hh * 512: c * D + hh * 512 + 512],
                        c == 0, False) for c in range(8)]
                mms.append((self.pb[hh], self.ones_row[0:2, 0:128],
                            self.bias_rows["bo"][0:2, hh * 512:hh * 512 + 512],
                            False, True))
                self.mmg(mms, reads=["w_wo", "fx1t", "ones", "bsl3"],
                         writes=[f"p{hh}"])
            self.dma("qa", self.scr[0],
                     self.ag_cuo_out[t // 2, (t % 2) * 128:(t % 2) * 128 + 128, :],
                     reads=["cuoout"], writes=["s0"])
            h = self.scr[1]
            for hh in range(2):
                sl = slice(hh * 512, hh * 512 + 512)
                self.tt("dve", h[:, sl], self.pb[hh], self.scr[0][:, sl],
                        ALU.add, reads=[f"p{hh}", "s0"], writes=["s1"])
            self.sigmoid(self.scr[2], "s2", h, "s1", self.scr[3], "s3",
                         self.scr[4], "s4", scale=2.0)
            self.ts("dve", h, self.scr[2], 2.0, -1.0, ALU.mult, ALU.add,
                    reads=["s2"], writes=["s1"])
            self.sigmoid(self.scr[2], "s2", h, "s1", self.scr[3], "s3",
                         self.scr[4], "s4")
            g1 = self.scr[2]
            self.tt("dve", g1, h, g1, ALU.mult, reads=["s1", "s2"], writes=["s2"])
            if "g1" in self.taps:
                self.dma("qs", self.taps["g1"][rt, :], g1, reads=["s2"],
                         writes=["tapg1"])
            self.dma("qs", self.scr[3], self.x1_dram[rt, :], reads=["x1d"],
                     writes=["s3"])
            self.dma("qa", self.scr[4], self.s1_dram[rt, :], reads=["s1d"],
                     writes=["s4"])
            self.tt("dve", self.scr[4], self.scr[4], g1, ALU.mult,
                    reads=["s4", "s2"], writes=["s4"])
            self.tt("dve", self.scr[4], self.scr[4], self.scr[3], ALU.add,
                    reads=["s4", "s3"], writes=["s4"])
            self.dma("qs", self.out[rt, :], self.scr[4], reads=["s4"],
                     writes=["outd"])

    def epilogue(self):
        self.sb.final_wait("sync", "qs")
        self.sb.final_wait("sync", "qa")
        self.sb.final_wait("sync", "qp")


def _consts():
    identF = np.eye(128, dtype=np.float32)
    identB = np.eye(128, dtype=ml_dtypes.bfloat16)
    sl = np.arange(128)
    triu = (sl[:, None] <= sl[None, :]).astype(ml_dtypes.bfloat16)
    causal = (sl[:, None] <= sl[None, :]).astype(np.float32)
    shd = (sl[:, None] == sl[None, :] - 1).astype(ml_dtypes.bfloat16)
    pick = np.zeros((128, 4), dtype=ml_dtypes.bfloat16)
    pick[:, 0] = 1.0
    pick[127, 1] = 1.0
    return identF, identB, triu, causal, shd, pick


_BUILT = {}


def _get_nc():
    key = (BUILD_UPTO, tuple(DEBUG_TAPS))
    if key not in _BUILT:
        _BUILT[key] = K().build()
    return _BUILT[key]


def kernel(**inputs):
    nc = _get_nc()
    identF, identB, triu, causal, shd, pick = _consts()
    tp = lambda a: np.ascontiguousarray(np.asarray(a).T)
    x = np.asarray(inputs["L0_attn_x"])
    base = {
        "Wq1T": tp(inputs["L0_attn_Wq1"]), "bq1": np.asarray(inputs["L0_attn_bq1"]).reshape(1, D),
        "Wk1T": tp(inputs["L0_attn_Wk1"]), "bk1": np.asarray(inputs["L0_attn_bk1"]).reshape(1, D),
        "Wk2T": tp(inputs["L0_attn_Wk2"]), "bk2": np.asarray(inputs["L0_attn_bk2"]).reshape(1, D),
        "WprojT": tp(inputs["L0_attn_Wproj"]), "bproj": np.asarray(inputs["L0_attn_bproj"]).reshape(1, D),
        "s_in": np.asarray(inputs["L0_ffn_s_in"]).reshape(1, 1),
        "s_in_down": np.asarray(inputs["L0_ffn_s_in_down"]).reshape(1, 1),
        "WqkvT": tp(inputs["L1_attn_W_qkv"]),
        "convw": np.ascontiguousarray(np.asarray(inputs["L1_attn_conv_w"]).reshape(CD, 4)),
        "WzT": tp(inputs["L1_attn_W_z"]),
        "WabT": np.ascontiguousarray(
            np.concatenate([np.asarray(inputs["L1_attn_W_b"]).T,
                            np.asarray(inputs["L1_attn_W_a"]).T], axis=1)),
        "dt_bias": np.asarray(inputs["L1_attn_dt_bias"]).reshape(1, H),
        "A_log": np.asarray(inputs["L1_attn_A_log"]).reshape(1, H),
        "WoutT": tp(inputs["L1_attn_W_out"]),
        "identF": identF, "identB": identB, "triuB": triu, "causalF": causal,
        "shdB": shd, "pickB": pick,
    }
    for n in ("wf", "wi", "wc", "wo", "uf", "ui", "uc", "uo"):
        base[n + "T"] = tp(inputs["L1_ffn_" + n])
    for n in ("bf", "bi", "bc", "bo", "in_c"):
        base["i_" + n] = np.asarray(inputs["L1_ffn_" + n]).reshape(1, D)
    gate_w = np.asarray(inputs["L0_ffn_gate_w"])
    up_w = np.asarray(inputs["L0_ffn_up_w"])
    down_w = np.asarray(inputs["L0_ffn_down_w"])
    in_maps = []
    for b in range(NCORE):
        m = dict(base)
        m["x"] = np.ascontiguousarray(x[b])
        m["gate_sh"] = np.ascontiguousarray(gate_w[b * 512:(b + 1) * 512])
        m["up_sh"] = np.ascontiguousarray(up_w[b * 512:(b + 1) * 512])
        m["down_sh"] = np.ascontiguousarray(down_w[b * 128:(b + 1) * 128])
        in_maps.append(m)
    res = run_bass_kernel_spmd(nc, in_maps, core_ids=list(range(NCORE)))
    out = np.stack([res.results[b]["out"] for b in range(NCORE)], axis=0)
    kernel.last_results = res
    return out
